# version 28
# baseline (speedup 1.0000x reference)
import contextlib
import numpy as np
import ml_dtypes
import concourse.bass as bass
import concourse.mybir as mybir
from concourse.bass_utils import run_bass_kernel_spmd

F32 = mybir.dt.float32
F16 = mybir.dt.float16
BF16 = mybir.dt.bfloat16
AF = mybir.ActivationFunctionType
ALU = mybir.AluOpType

D = 1024
DI = 2048
T = 2048
NST = 16
RK = 64
H = 4096
L = 256
NCHUNK = T // L
KD = D // 128
KDI = DI // 128
KH = H // 128
EPS = 1e-5

ENGS = ("P", "V", "A")
QUEUES = ("Q", "G")

_EM_BOX = []


class Handle:
    __slots__ = ("eng", "pos")

    def __init__(self, eng, pos):
        self.eng = eng
        self.pos = pos


def hmax(*hs):
    hs = [h for h in hs if h is not None]
    assert hs and all(h.eng == hs[0].eng for h in hs)
    return max(hs, key=lambda h: h.pos)


class Emitter:

    def __init__(self):
        self.seq = {e: [] for e in ENGS + QUEUES}
        self.bucket = {e: [] for e in ENGS}
        self.recs = {e: [] for e in ENGS}
        self.scnt = {}
        self.gser = 0

    def emit(self, eng, fn, waits=(), inc=1, aux=False):
        rec = {"fn": fn, "waits": [w for w in waits if w is not None],
               "inc": inc, "stream": None, "ser": self.gser}
        self.gser += 1
        self.recs[eng].append(rec)
        h = Handle(eng, len(self.recs[eng]) - 1)
        rec["h"] = h
        (self.bucket[eng] if aux else self.seq[eng]).append(rec)
        return h

    def splice(self, eng):
        self.seq[eng].extend(self.bucket[eng])
        del self.bucket[eng][:]

    def splice_upto(self, eng, ser):
        b = self.bucket[eng]
        n = 0
        while n < len(b) and b[n]["ser"] <= ser:
            n += 1
        self.seq[eng].extend(b[:n])
        del b[:n]

    def dma(self, queue, stream, fn, waits=()):
        rec = {"fn": fn, "waits": [w for w in waits if w is not None],
               "inc": 16, "stream": stream}
        self.scnt[stream] = self.scnt.get(stream, 0) + 16
        self.seq[queue].append(rec)
        return self.scnt[stream]

    def finalize(self):
        for e in ENGS:
            self.splice(e)
            c = 0
            for i, rec in enumerate(self.seq[e]):
                c += rec["inc"]
                rec["cnt"] = c
                rec["ord"] = i
        for e in ENGS + QUEUES:
            for rec in self.seq[e]:
                w = {}
                for item in rec["waits"]:
                    if isinstance(item, Handle):
                        tgt = self.recs[item.eng][item.pos]
                        assert "cnt" in tgt, "waited handle not in seq"
                        assert tgt["inc"] > 0, "waited handle has inc=0"
                        if item.eng == e and e not in QUEUES:
                            assert tgt["ord"] < rec["ord"], \
                                f"same-engine wait out of order on {e}"
                            continue
                        if tgt["cnt"] > w.get(item.eng, 0):
                            w[item.eng] = tgt["cnt"]
                    else:
                        s, c2 = item
                        if c2 > 0 and c2 > w.get(s, 0):
                            w[s] = c2
                rec["rwaits"] = sorted(w.items())


def build():
    nc = bass.Bass("TRN2", target_bir_lowering=False)

    x_d = nc.dram_tensor("x", [D, T], F32, kind="ExternalInput").ap()
    w_in_d = nc.dram_tensor("w_in", [D, 2 * DI], F32, kind="ExternalInput").ap()
    conv_w_d = nc.dram_tensor("conv_w", [DI, 4], F32, kind="ExternalInput").ap()
    conv_b_d = nc.dram_tensor("conv_b", [16, 128], F32, kind="ExternalInput").ap()
    w_x_d = nc.dram_tensor("w_x", [DI, RK + 2 * NST], F32, kind="ExternalInput").ap()
    w_dt_d = nc.dram_tensor("w_dt", [RK, DI], F32, kind="ExternalInput").ap()
    b_dt_d = nc.dram_tensor("b_dt", [16, 128], F32, kind="ExternalInput").ap()
    a_log_d = nc.dram_tensor("a_log", [DI, NST], F32, kind="ExternalInput").ap()
    dvec_d = nc.dram_tensor("dvec", [16, 128], F32, kind="ExternalInput").ap()
    w_out_d = nc.dram_tensor("w_out", [DI, D], F32, kind="ExternalInput").ap()
    w1t_d = nc.dram_tensor("w1t", [D, H], F32, kind="ExternalInput").ap()
    w1r_d = nc.dram_tensor("w1r", [H, D], F32, kind="ExternalInput").ap()
    w2r_d = nc.dram_tensor("w2r", [D, H], F32, kind="ExternalInput").ap()
    b1_d = nc.dram_tensor("b1", [32, 128], F32, kind="ExternalInput").ap()
    w2t_d = nc.dram_tensor("w2t", [H, D], F32, kind="ExternalInput").ap()
    b2_d = nc.dram_tensor("b2", [8, 128], F32, kind="ExternalInput").ap()
    eye_d = nc.dram_tensor("eye", [32, 32], F32, kind="ExternalInput").ap()
    out_d = nc.dram_tensor("out", [D, T], F32, kind="ExternalOutput").ap()
    w_in16_d = nc.dram_tensor("w_in16", [D, 2 * DI], F16, kind="Internal").ap()
    w_out16_d = nc.dram_tensor("w_out16", [DI, D], F16, kind="Internal").ap()
    w1s_d = nc.dram_tensor("w1s", [D, H], F16, kind="Internal").ap()
    w2s_d = nc.dram_tensor("w2s", [H, D], F16, kind="Internal").ap()
    bounce_d = nc.dram_tensor("bounce", [4, 2 * L], F32, kind="Internal").ap()
    bounce16_d = nc.dram_tensor("bounce16", [32, L], F16, kind="Internal").ap()

    em = Emitter()
    ready = {}
    chst = {}

    es = contextlib.ExitStack()
    with es:
        def sbt(name, shape, dt):
            return es.enter_context(nc.sbuf_tensor(name, shape, dt))

        wring = sbt("wring", [128, 2 * 4096], F16)
        wx16 = sbt("wx16", [128, KDI * 96], F16)
        wdt16 = sbt("wdt16", [64, KDI * 128], F16)
        cw = sbt("cw", [128, KDI * 4], F32)
        cb = sbt("cb", [128, KDI], F32)
        bdt = sbt("bdt", [128, KDI], F32)
        dpv = sbt("dpv", [128, KDI], F32)
        asc = sbt("asc", [128, KDI * NST], F32)
        b1c = sbt("b1c", [128, KH], F32)
        b2c = sbt("b2c", [128, KD], F32)
        al1 = sbt("al1", [128, KH], F32)
        al2 = sbt("al2", [128, KD], F32)
        al2p = sbt("al2p", [128, 4 * KD], F32)
        eye_s = sbt("eye_s", [32, 32], F32)
        ones32 = sbt("ones32", [128, 1], F32)
        ones16 = sbt("ones16", [128, 1], F16)
        cst = sbt("cst", [128, 6], F32)

        xr = sbt("xr", [128, 2 * KD * L], F32)
        xn = sbt("xn", [128, KD * L], F16)
        xi = sbt("xi", [128, KDI * (L + 4)], F16)
        xcv = sbt("xcv", [128, 2 * KDI * L], F16)
        szv = sbt("szv", [128, 2 * KDI * L], F16)
        dlt = sbt("dlt", [128, 2 * KDI * L], F16)
        uv = sbt("uv", [128, 2 * KDI * L], F16)
        dtl = sbt("dtl", [64, L], F16)
        bcst = sbt("bcst", [32, L], F16)
        bbc = sbt("bbc", [128, 2 * NST * L], F16)
        cbc = sbt("cbc", [128, 2 * NST * L], BF16)
        QL = 4 * (L + 1)
        dar = sbt("dar", [128, 3 * QL], F16)
        dbu = sbt("dbu", [128, QL], F16)
        hall = sbt("hall", [128, 4 * QL], BF16)
        yall = sbt("yall", [128, 4 * QL], F16)
        x2v = sbt("x2v", [128, KD * L], F32)
        hh = sbt("hh", [128, KH * L], F16)
        ffbuf = sbt("ffbuf", [128, 2 * L], F16)
        lnbc = sbt("lnbc", [128, 2 * L], F32)
        state = sbt("state", [128, NST * KDI], F32)
        r_pair = sbt("r_pair", [1, 2 * L], F32)
        r_sum = sbt("r_sum", [1, L], F32)
        r_sq = sbt("r_sq", [1, L], F32)

        s16a = [hh[:, 0:2048], hh[:, 2048:4096]]
        s16b = [hh[:, 4096:6144], hh[:, 6144:8192]]

        psmm = [es.enter_context(nc.psum_tensor(f"psmm{i}", [128, 512], F32))
                for i in range(4)]
        psX = es.enter_context(nc.psum_tensor("psX", [128, 512], F32))
        psS = es.enter_context(nc.psum_tensor("psS", [128, 512], F32))
        psE = es.enter_context(nc.psum_tensor("psE", [128, 512], F32))
        psE2 = es.enter_context(nc.psum_tensor("psE2", [128, 512], F32))

        block = es.enter_context(nc.Block())

        def v3(t, k):
            return t[:].rearrange("p (a b) -> p a b", a=k)

        xr4 = v3(xr, 2 * KD)
        xn3 = v3(xn, KD)
        xi3 = v3(xi, KDI)
        xcv4 = v3(xcv, 2 * KDI)
        szv4 = v3(szv, 2 * KDI)
        dlt4 = v3(dlt, 2 * KDI)
        uv4 = v3(uv, 2 * KDI)
        bbc4 = v3(bbc, 2 * NST)
        cbc4 = v3(cbc, 2 * NST)
        hall4 = hall[:].rearrange("p (q s t) -> p q s t", q=4, s=4)

        def hof(dch):
            return (dch // 4) * QL + (dch % 4) * (L + 1)
        x2v3 = v3(x2v, KD)
        hh3 = v3(hh, KH)
        lnbc3 = v3(lnbc, 2)
        ffb3 = v3(ffbuf, 2)
        cw3 = v3(cw, KDI)
        asc3 = v3(asc, KDI)
        wx3 = v3(wx16, KDI)
        wdt3 = wdt16[:].rearrange("k (a m) -> k a m", a=KDI)
        x_src = x_d.rearrange("(a p) t -> p a t", p=128)
        out_dst = out_d.rearrange("(a p) t -> p a t", p=128)

        def sl(view4, c, k):
            off = (c % 2) * k
            return view4[:, off:off + k, :]

        slot_free = [None] * 4
        slot_rr = [0]

        def acquire_slot():
            s = slot_rr[0] % 4
            slot_rr[0] += 1
            return s, psmm[s][:, 0:256], slot_free[s]

        ring_free = [None, None]
        ring_rr = [0]

        E = em.emit
        DMA = em.dma

        E("V", lambda e: e.memset(ones32[:], 1.0))
        E("V", lambda e: e.memset(ones16[:], 1.0))
        E("V", lambda e: e.memset(cst[:, 0:1], EPS))
        E("V", lambda e: e.memset(cst[:, 1:2], 1.0))
        E("V", lambda e: e.memset(cst[:, 2:3], -0.5))
        E("V", lambda e: e.memset(cst[:, 3:4], 1.0 / 1024))
        E("V", lambda e: e.memset(cst[:, 4:5], 1.0 / 4096))
        E("V", lambda e: e.memset(state[:], 0.0))
        E("V", lambda e: e.memset(xi[:], 0.0))
        E("V", lambda e: e.memset(dar[:], 0.0))
        DMA("Q", "sq", lambda e: e.dma_start(out=eye_s[:], in_=eye_d))
        DMA("Q", "sq", lambda e: e.dma_start(
            out=cw3, in_=conv_w_d.rearrange("(a p) k -> p a k", p=128)))
        mSqA = DMA("Q", "sq", lambda e: e.dma_start(
            out=asc3, in_=a_log_d.rearrange("(a p) n -> p a n", p=128)))
        SQ = ("sq", mSqA)
        DMA("Q", "sv", lambda e: e.dma_start(out=x2v[0:16, 0:128],
                                             in_=conv_b_d))
        DMA("Q", "sv", lambda e: e.dma_start(out=x2v[0:16, 128:256],
                                             in_=b_dt_d))
        DMA("Q", "sv", lambda e: e.dma_start(out=x2v[0:16, 256:384],
                                             in_=dvec_d))
        DMA("Q", "sv", lambda e: e.dma_start(out=x2v[0:32, 384:512],
                                             in_=b1_d))
        mVl = DMA("Q", "sv", lambda e: e.dma_start(out=x2v[0:8, 512:640],
                                                   in_=b2_d))
        SV = ("sv", mVl)
        E("P", lambda e: e.transpose(out=psS[:, 0:16], in_=x2v[0:16, 0:128],
                                     identity=eye_s[0:16, 0:16]),
          waits=[SQ, SV], inc=0)
        E("P", lambda e: e.transpose(out=psS[:, 16:32], in_=x2v[0:16, 128:256],
                                     identity=eye_s[0:16, 0:16]), inc=0)
        E("P", lambda e: e.transpose(out=psS[:, 32:48], in_=x2v[0:16, 256:384],
                                     identity=eye_s[0:16, 0:16]), inc=0)
        E("P", lambda e: e.transpose(out=psS[:, 48:80], in_=x2v[0:32, 384:512],
                                     identity=eye_s[0:32, 0:32]), inc=0)
        mT5 = E("P", lambda e: e.transpose(out=psS[:, 80:88],
                                           in_=x2v[0:8, 512:640],
                                           identity=eye_s[0:8, 0:8]), inc=1)
        E("V", lambda e: e.tensor_copy(out=cb[:], in_=psS[:, 0:16]),
          waits=[mT5])
        E("V", lambda e: e.tensor_copy(out=bdt[:], in_=psS[:, 16:32]))
        E("V", lambda e: e.tensor_copy(out=dpv[:], in_=psS[:, 32:48]))
        E("V", lambda e: e.tensor_copy(out=b1c[:], in_=psS[:, 48:80]))
        mVecT = E("V", lambda e: e.tensor_copy(out=b2c[:], in_=psS[:, 80:88]))
        ready["vecT"] = mVecT
        mAexp = E("A", lambda e: e.activation(out=asc[:], in_=asc[:],
                                              func=AF.Exp), waits=[SQ])
        E("V", lambda e: e.tensor_scalar_mul(out=asc[:], in0=asc[:],
                                             scalar1=-1.0), waits=[mAexp])
        DMA("G", "pg", lambda e: e.dma_start(
            out=wx3, in_=w_x_d.rearrange("(a p) m -> p a m", p=128)))
        mPg = DMA("G", "pg", lambda e: e.dma_start(
            out=wdt3, in_=w_dt_d.rearrange("k (a m) -> k a m", a=KDI)))
        PG = ("pg", mPg)

        def cast_weight(src_d, dst_d, pieces, cols):
            nsub = max(1, cols // 4096)
            csz = min(cols, 4096)
            for pc in range(pieces):
                for sub in range(nsub):
                    i = pc * nsub + sub
                    s = i % 2
                    ap = wring[:, s * 4096:s * 4096 + csz]
                    mG = DMA("G", f"cl{s}", lambda e, ap=ap, pc=pc, sub=sub:
                             e.dma_start(out=ap,
                                         in_=src_d[pc * 128:(pc + 1) * 128,
                                                   sub * csz:(sub + 1) * csz]),
                             waits=[(f"cs{s}", em.scnt.get(f"cs{s}", 0))])
                    DMA("Q", f"cs{s}", lambda e, ap=ap, pc=pc, sub=sub:
                        e.dma_start(out=dst_d[pc * 128:(pc + 1) * 128,
                                              sub * csz:(sub + 1) * csz],
                                    in_=ap),
                        waits=[(f"cl{s}", mG)])

        cast_weight(w_in_d, w_in16_d, KD, 2 * DI)
        CI = [("cs0", em.scnt.get("cs0", 0)), ("cs1", em.scnt.get("cs1", 0))]

        pstate = {}
        prep_units = []

        def unit_wout(i):
            def go():
                sb = i % 2
                mL = DMA("G", f"ol{sb}", lambda e, i=i, sb=sb: e.dma_start(
                    out=s16b[sb][:, 1024:2048],
                    in_=w_out_d[i * 128:(i + 1) * 128, :]),
                    waits=[(f"os{sb}", em.scnt.get(f"os{sb}", 0))])
                DMA("Q", f"os{sb}", lambda e, i=i, sb=sb: e.dma_start(
                    out=w_out16_d[i * 128:(i + 1) * 128, :],
                    in_=s16b[sb][:, 1024:2048]),
                    waits=[(f"ol{sb}", mL)])
            return go

        def unit_bin(src_d, dst_d, kc, qtr, s):
            def go():
                mL = DMA("Q", f"bl{s}", lambda e: e.dma_start(
                    out=x2v[:, s * 1024:(s + 1) * 1024],
                    in_=src_d[kc * 128:(kc + 1) * 128,
                              qtr * 1024:(qtr + 1) * 1024]),
                    waits=[pstate.get(("sgn", s)), mT5])
                mS = E("A", lambda e: e.activation(
                    out=s16a[s][:, 0:1024], in_=x2v[:, s * 1024:(s + 1) * 1024],
                    func=AF.Sign),
                    waits=[(f"bl{s}", mL),
                           (f"bs{s}", em.scnt.get(f"bs{s}", 0))], aux=True)
                pstate[("sgn", s)] = mS
                pstate["sgn_last"] = mS
                DMA("Q", f"bs{s}", lambda e: e.dma_start(
                    out=dst_d[kc * 128:(kc + 1) * 128,
                              qtr * 1024:(qtr + 1) * 1024],
                    in_=s16a[s][:, 0:1024]),
                    waits=[mS])
            return go

        def unit_alp(dst_ap, src_row):
            def go(dst_ap=dst_ap, src_row=src_row):
                sb = pstate.get("alp_i", 0) % 2
                pstate["alp_i"] = pstate.get("alp_i", 0) + 1
                mL = DMA("G", f"ar{sb}", lambda e: e.dma_start(
                    out=s16b[sb][:, 0:1024], in_=src_row),
                    waits=[pstate.get(("red", sb))])
                mA = E("A", lambda e: e.activation(
                    out=s16b[sb][:, 0:1024], in_=s16b[sb][:, 0:1024],
                    func=AF.Abs), waits=[(f"ar{sb}", mL)], aux=True)
                mR = E("V", lambda e: e.reduce_sum(
                    out=dst_ap, in_=s16b[sb][:, 0:1024],
                    axis=mybir.AxisListType.X), waits=[mA], aux=True)
                pstate[("red", sb)] = mR
            return go

        def unit_alpha_final():
            def go():
                al2p3 = al2p[:].rearrange("p (r q) -> p r q", q=4)
                E("V", lambda e: e.tensor_tensor(
                    out=al2[:], in0=al2p3[:, :, 0], in1=al2p3[:, :, 1],
                    op=ALU.add),
                    waits=[pstate[("red", 0)], pstate[("red", 1)]], aux=True)
                E("V", lambda e: e.tensor_tensor(
                    out=al2p3[:, :, 0], in0=al2p3[:, :, 2],
                    in1=al2p3[:, :, 3], op=ALU.add), aux=True)
                E("V", lambda e: e.tensor_tensor(
                    out=cst[:, 5:6], in0=cst[:, 1:2], in1=cst[:, 1:2],
                    op=ALU.mult), inc=0, aux=True)
                E("V", lambda e: e.tensor_tensor(
                    out=cst[:, 5:6], in0=cst[:, 1:2], in1=cst[:, 1:2],
                    op=ALU.mult), inc=0, aux=True)
                mAd = E("V", lambda e: e.tensor_tensor(
                    out=al2[:], in0=al2[:], in1=al2p3[:, :, 0],
                    op=ALU.add), aux=True)
                mS1 = E("A", lambda e: e.activation(
                    out=al1[:], in_=al1[:], func=AF.Identity,
                    scale=cst[:, 3:4]), waits=[mAd], aux=True)
                mS2 = E("A", lambda e: e.activation(
                    out=al2[:], in_=al2[:], func=AF.Identity,
                    scale=cst[:, 4:5]), aux=True)
                ready["alpha"] = mS2
                ready["alpha_v"] = mAd
            return go

        LW = [unit_wout(i) for i in range(KDI)]
        LA = [unit_alp(al1[:, r:r + 1], w1r_d[r * 128:(r + 1) * 128, :])
              for r in range(KH)]
        LA += [unit_alp(al2p[:, r * 4 + q:r * 4 + q + 1],
                        w2r_d[r * 128:(r + 1) * 128,
                              q * 1024:(q + 1) * 1024])
               for r in range(KD) for q in range(4)]
        LA.append(unit_alpha_final())
        LB = []
        nbin = 0
        for kc in range(KD):
            for qtr in range(4):
                LB.append(unit_bin(w1t_d, w1s_d, kc, qtr, nbin % 2))
                nbin += 1
        for kc in range(KH):
            LB.append(unit_bin(w2t_d, w2s_d, kc, 0, nbin % 2))
            nbin += 1
        tagged = []
        for lst in (LW, LA, LB):
            for j, u in enumerate(lst):
                tagged.append(((j + 0.5) / len(lst), len(tagged), u))
        tagged.sort()
        prep_units = [t[2] for t in tagged]
        pidx = [0]

        def drain_prep(k):
            n = min(k, len(prep_units) - pidx[0])
            for i in range(n):
                prep_units[pidx[0] + i]()
            pidx[0] += n

        def stream_mm(src16_d, mg_cols, rhs3, rhs_wait, evac_fn, mg_count,
                      kparts, src_waits):
            src3 = src16_d.rearrange("(a p) m -> p a m", p=128)
            for mg in range(mg_count):
                s = ring_rr[0] % 2
                ring_rr[0] += 1
                ap = wring[:, s * 4096:(s + 1) * 4096]
                stream = "wr%d" % s
                mLd = DMA("Q", stream, lambda e, mg=mg, ap=ap: e.dma_start(
                    out=ap[:, 0:kparts * mg_cols].rearrange(
                        "p (a m) -> p a m", a=kparts),
                    in_=src3[:, :, mg * mg_cols:(mg + 1) * mg_cols]),
                    waits=[ring_free[s]] + src_waits)
                n_m = mg_cols // 128
                for ml in range(n_m):
                    m = mg * n_m + ml
                    slq, pap, sfree = acquire_slot()
                    hmm = None
                    for kc in range(kparts):
                        off = kc * mg_cols + ml * 128
                        hmm = E("P", lambda e, ap=ap, off=off, pap=pap, kc=kc:
                                e.matmul(out=pap, lhsT=ap[:, off:off + 128],
                                         rhs=rhs3[:, kc, :],
                                         start=(kc == 0),
                                         stop=(kc == kparts - 1)),
                                waits=([(stream, mLd), rhs_wait, sfree]
                                       if kc == 0 else []),
                                inc=(1 if kc == kparts - 1 else 0))
                    slot_free[slq] = evac_fn(m, pap, hmm)
                ring_free[s] = hmm

        def layer_norm(xsrc3, kd, out3, brow0, xwait, war_xn, aux):
            mSq = E("A", lambda e, xsrc3=xsrc3: e.activation(
                out=xn3, in_=xsrc3, func=AF.Square),
                waits=[xwait, ready.get("t1_done"), war_xn], aux=aux)
            for kc in range(kd):
                E("P", lambda e, kc=kc, xsrc3=xsrc3: e.matmul(
                    out=psS[0:1, 0:L], lhsT=ones32[:], rhs=xsrc3[:, kc, :],
                    start=(kc == 0), stop=(kc == kd - 1)),
                    waits=([xwait, ready.get("psS_war", ready["vecT"])]
                           if kc == 0 else []), inc=0)
            hstat = None
            for kc in range(kd):
                hstat = E("P", lambda e, kc=kc: e.matmul(
                    out=psS[0:1, 256:256 + L], lhsT=ones16[:],
                    rhs=xn3[:, kc, :],
                    start=(kc == 0), stop=(kc == kd - 1)),
                    waits=([mSq] if kc == 0 else []),
                    inc=(1 if kc == kd - 1 else 0))
            E("V", lambda e: e.tensor_copy(out=r_sum[:], in_=psS[0:1, 0:L]),
              waits=[hstat, ("rp", ready.get("rows_war", 0))], aux=aux)
            mRows = E("V", lambda e: e.tensor_copy(
                out=r_sq[:], in_=psS[0:1, 256:256 + L]), aux=aux)
            ready["psS_war"] = mRows
            E("V", lambda e: e.tensor_scalar_mul(out=r_sum[:], in0=r_sum[:],
                                                 scalar1=1.0 / D), aux=aux)
            E("V", lambda e: e.tensor_mul(out=r_pair[:, 0:L], in0=r_sum[:],
                                          in1=r_sum[:]), aux=aux)
            mVar = E("V", lambda e: e.scalar_tensor_tensor(
                out=r_sq[:], in0=r_sq[:], scalar=1.0 / D,
                in1=r_pair[:, 0:L],
                op0=ALU.mult, op1=ALU.subtract), aux=aux)
            E("A", lambda e: e.activation(out=r_sq[:], in_=r_sq[:],
                                          func=AF.Ln, bias=cst[0:1, 0:1]),
              waits=[mVar], aux=aux)
            mRstd = E("A", lambda e: e.activation(out=r_pair[:, 0:L],
                                                  in_=r_sq[:],
                                                  func=AF.Exp,
                                                  scale=cst[0:1, 2:3]),
                      aux=aux)
            mMr = E("V", lambda e: e.tensor_mul(out=r_pair[:, L:2 * L],
                                                in0=r_sum[:],
                                                in1=r_pair[:, 0:L]),
                    waits=[mRstd], aux=aux)
            mRo = DMA("G", "ro", lambda e, brow0=brow0: e.dma_start(
                out=bounce_d[brow0:brow0 + 1, :], in_=r_pair[:]),
                waits=[mMr, mRstd])
            mRep = DMA("G", "rp", lambda e, brow0=brow0: e.dma_start(
                out=lnbc[:],
                in_=bounce_d[brow0:brow0 + 1, :].partition_broadcast(128)),
                waits=[("ro", mRo), ("rp", em.scnt.get("rp", 0)),
                       ready.get("lnbc_war")])
            ready["rows_war"] = mRep
            E("V", lambda e, xsrc3=xsrc3, kd=kd: e.tensor_tensor(
                out=xn3, in0=xsrc3,
                in1=lnbc3[:, 0:1, :].broadcast_to([128, kd, L]),
                op=ALU.mult),
                waits=[("rp", mRep)], aux=aux)
            mXn = E("V", lambda e, out3=out3, kd=kd: e.tensor_tensor(
                out=out3, in0=xn3,
                in1=lnbc3[:, 1:2, :].broadcast_to([128, kd, L]),
                op=ALU.subtract), aux=aux)
            ready["lnbc_war"] = mXn
            ready["t1_done"] = mXn
            return mXn

        scan_ms_all = []
        dA_all = []
        last = {"psX_war": None, "bcst_war": 0, "gall_war": None,
                "x2_war": None, "hh_war": None, "xn_war": None,
                "ffb_war": {}}

        def emit_front(c):
            st = chst.setdefault(c, {})
            p2 = chst.get(c - 2, {})
            p1 = chst.get(c - 1, {})
            xv3 = sl(xr4, c, KD)
            mX = DMA("Q", "xl", lambda e, c=c: e.dma_start(
                out=sl(xr4, c, KD), in_=x_src[:, :, c * L:c * L + L]),
                waits=[p2.get("resid")])
            mXn = layer_norm(xv3, KD, xn3, 0, ("xl", mX), last["xn_war"],
                             aux=True)
            st["xn"] = mXn

            evac_ms = {}

            def evac_win(m, pap, hmm):
                if m < KDI:
                    E("A", lambda e, m=m, pap=pap: e.activation(
                        out=xi3[:, m, 4:4 + L], in_=pap, func=AF.Exp),
                        waits=[hmm, p1.get("hist")], aux=True)
                    mE = E("A", lambda e, m=m: e.activation(
                        out=xi3[:, m, 4:4 + L], in_=xi3[:, m, 4:4 + L],
                        func=AF.Ln), aux=True)
                else:
                    mE = E("A", lambda e, m=m, pap=pap, c=c: e.activation(
                        out=sl(szv4, c, KDI)[:, m - KDI, :], in_=pap,
                        func=AF.Silu),
                        waits=[hmm, p2.get("gate")], aux=True)
                evac_ms[m] = mE
                return mE

            stream_mm(w_in16_d, 512, xn3, mXn, evac_win, 8, KD, CI)
            last["xn_war"] = em.recs["P"][-1]["h"]

            xc3c = sl(xcv4, c, KDI)
            acc_ms = {}
            for dch in range(KDI):
                for k in range(4):
                    if k == 0:
                        wv = [evac_ms[dch], SQ,
                              (p2.get("gate") if dch == 0 else None)]
                        E("V", lambda e, dch=dch, xc3c=xc3c:
                          e.tensor_scalar_mul(
                              out=xc3c[:, dch, :],
                              in0=xi3[:, dch, 1:1 + L],
                              scalar1=cw3[:, dch, 0:1]), waits=wv, inc=0,
                          aux=True)
                    else:
                        E("V", lambda e, dch=dch, k=k, xc3c=xc3c:
                          e.scalar_tensor_tensor(
                              out=xc3c[:, dch, :],
                              in0=xi3[:, dch, 1 + k:1 + k + L],
                              scalar=cw3[:, dch, k:k + 1],
                              in1=xc3c[:, dch, :],
                              op0=ALU.mult, op1=ALU.add),
                          inc=(1 if k == 3 else 0), aux=True)
                acc_ms[dch] = em.recs["V"][-1]["h"]
            for dch in range(KDI):
                mXc = E("A", lambda e, dch=dch, xc3c=xc3c:
                        e.activation(out=xc3c[:, dch, :],
                                     in_=xc3c[:, dch, :], func=AF.Silu,
                                     bias=cb[:, dch:dch + 1]),
                        waits=[acc_ms[dch]], aux=True)
                evac_ms[("xc", dch)] = mXc
            mXcAll = hmax(*[evac_ms[("xc", d_)] for d_ in range(KDI)])
            st["szall"] = hmax(*[evac_ms[m] for m in range(KDI, 2 * KDI)])
            mHist = E("V", lambda e: e.tensor_copy(
                out=xi3[:, :, 1:4], in_=xi3[:, :, 1 + L:4 + L]),
                waits=[hmax(*[evac_ms[m] for m in range(KDI)])], aux=True)
            st["hist"] = mHist

            hxd = None
            for kc in range(KDI):
                hxd = E("P", lambda e, kc=kc, xc3c=xc3c: e.matmul(
                    out=psX[0:96, 0:L], lhsT=wx3[:, kc, :],
                    rhs=xc3c[:, kc, :],
                    start=(kc == 0), stop=(kc == KDI - 1)),
                    waits=([mXcAll, PG, last["psX_war"]] if kc == 0 else []),
                    inc=(1 if kc == KDI - 1 else 0))
            dtlc = dtl[:, 0:L]
            E("A", lambda e, dtlc=dtlc: e.activation(
                out=dtlc, in_=psX[0:64, 0:L], func=AF.Exp),
                waits=[hxd, p1.get("wdt_end")], aux=True)
            mDtl = E("A", lambda e, dtlc=dtlc: e.activation(
                out=dtlc, in_=dtlc, func=AF.Ln), aux=True)
            E("A", lambda e: e.activation(out=bcst[:],
                                          in_=psX[64:96, 0:L],
                                          func=AF.Exp),
              waits=[("bo", last["bcst_war"])], aux=True)
            mBcst = E("A", lambda e: e.activation(out=bcst[:], in_=bcst[:],
                                                  func=AF.Ln), aux=True)
            last["psX_war"] = mBcst
            mBo = DMA("G", "bo", lambda e: e.dma_start(
                out=bounce16_d, in_=bcst[:]),
                waits=[mBcst, ("rp", p1.get("crep", 0))])
            last["bcst_war"] = mBo
            DMA("G", "rp", lambda e, c=c: e.dma_start(
                out=sl(bbc4, c, NST),
                in_=bounce16_d[0:16, :].partition_broadcast(128)),
                waits=[("bo", mBo), ("rp", em.scnt.get("rp", 0)),
                       p2.get("gate")])
            mCrep = DMA("G", "rp", lambda e, c=c: e.dma_start(
                out=sl(cbc4, c, NST),
                in_=bounce16_d[16:32, :].partition_broadcast(128)),
                waits=[("rp", em.scnt.get("rp", 0)), p2.get("gate")])
            st["crep"] = mCrep

            dlt3c = sl(dlt4, c, KDI)
            hdt = None
            for dch in range(KDI):
                pap = (psE if dch % 2 == 0 else psE2)[:, 0:L]
                hdt = E("P", lambda e, pap=pap, dtlc=dtlc, dch=dch: e.matmul(
                    out=pap, lhsT=wdt3[:, dch, :], rhs=dtlc,
                    start=True, stop=True),
                    waits=[mDtl, PG, ready.get(("sp", c, dch - 2))], inc=1)
                mEx = E("A", lambda e, pap=pap, dch=dch, dlt3c=dlt3c:
                        e.activation(out=dlt3c[:, dch, :], in_=pap,
                                     func=AF.Exp, bias=bdt[:, dch:dch + 1]),
                        waits=[hdt, (p2.get("gate") if dch == 0 else None)],
                        aux=True)
                ready[("sp", c, dch)] = mEx
            for dch in range(KDI):
                mDl = E("A", lambda e, dch=dch, dlt3c=dlt3c: e.activation(
                    out=dlt3c[:, dch, :], in_=dlt3c[:, dch, :], func=AF.Ln,
                    bias=cst[:, 1:2]), aux=True)
            st["dltall"] = mDl
            st["wdt_end"] = hdt

            mU = E("V", lambda e, c=c: e.tensor_tensor(
                out=sl(uv4, c, KDI), in0=sl(dlt4, c, KDI),
                in1=sl(xcv4, c, KDI), op=ALU.mult),
                waits=[st["dltall"]], aux=True)
            st["mU"] = mU
            st["eser"] = em.gser

        def emit_scan(c):
            st = chst[c]
            base_g = c * 256
            dlt3c = sl(dlt4, c, KDI)
            uv3c = sl(uv4, c, KDI)
            bbc3c = sl(bbc4, c, NST)
            cbc3c = sl(cbc4, c, NST)
            xc3c = sl(xcv4, c, KDI)

            em.splice_upto("V", st["eser"])
            em.splice_upto("A", st["eser"])

            sers = sorted(r["ser"] for e2 in ("V", "A")
                          for r in em.bucket[e2])

            def frontier(k):
                if not sers:
                    return -1
                idx = min(len(sers), (len(sers) * k) // 64)
                return sers[idx - 1] if idx > 0 else -1

            sz3c = sl(szv4, c, KDI)
            base_q = c * 64

            def emit_dA(gq):
                lq = gq % 64
                n_, j_ = divmod(lq, 4)
                rs = gq % 3
                wv = [scan_ms_all[gq - 3] if gq >= 3 else st["mU"]]
                if lq == 0:
                    wv.append(st["dltall"])
                m = None
                for s in range(4):
                    off = rs * QL + s * (L + 1)
                    m = E("A", lambda e, n_=n_, dch_=j_ * 4 + s, off=off:
                          e.activation(out=dar[:, off:off + L],
                                       in_=dlt3c[:, dch_, :], func=AF.Exp,
                                       scale=asc3[:, dch_, n_:n_ + 1]),
                          waits=(wv if s == 0 else []))
                dA_all.append(m)

            def emit_dbu(gq):
                lq = gq % 64
                n_, j_ = divmod(lq, 4)
                col0 = n_ * KDI + j_ * 4
                wv = [("rp", st["crep"]), st["mU"]] if lq == 0 else []
                E("V", lambda e, col0=col0: e.tensor_copy(
                    out=dbu[:].rearrange(
                        "p (s t) -> p s t", s=4)[:, 0:3, 256:257].rearrange(
                        "p a b -> p (a b)"),
                    in_=state[:, col0 + 1:col0 + 4]), waits=wv, inc=0)
                for s in range(4):
                    off = s * (L + 1)
                    E("V", lambda e, n_=n_, dch_=j_ * 4 + s, off=off:
                      e.tensor_tensor(
                          out=dbu[:, off:off + L], in0=uv3c[:, dch_, :],
                          in1=bbc3c[:, n_, :], op=ALU.mult), inc=0)

            for i in range(3):
                emit_dA(base_q + i)
            emit_dbu(base_q)
            for n in range(NST):
                for j in range(4):
                    gq = base_q + n * 4 + j
                    mScan = E("V", lambda e, j=j, rs=(gq % 3),
                              col=n * KDI + j * 4:
                              e.tensor_tensor_scan(
                                  out=hall[:, j * QL:(j + 1) * QL],
                                  data0=dar[:, rs * QL:(rs + 1) * QL],
                                  data1=dbu[:],
                                  initial=state[:, col:col + 1],
                                  op0=ALU.mult, op1=ALU.add),
                              waits=[dA_all[gq]])
                    scan_ms_all.append(mScan)
                    if gq + 1 < base_q + 64:
                        emit_dbu(gq + 1)
                    if gq + 3 < base_q + 64:
                        emit_dA(gq + 3)
                    f = frontier(n * 4 + j + 1)
                    em.splice_upto("V", f)
                    em.splice_upto("A", f)
                E("V", lambda e, n=n: e.tensor_copy(
                    out=state[:, n * KDI:n * KDI + 12],
                    in_=hall4[:, 0:3, :, 255:256].rearrange(
                        "p a b c -> p (a b c)")), inc=0)
                for dch in range(12):
                    E("V", lambda e, dch=dch, n=n, o=hof(dch):
                      e.tensor_tensor(
                          out=hall[:, o:o + L], in0=hall[:, o:o + L],
                          in1=cbc3c[:, n, :], op=ALU.mult), inc=0)
                E("V", lambda e, n=n: e.tensor_copy(
                    out=state[:, n * KDI + 12:(n + 1) * KDI],
                    in_=hall4[:, 3:4, :, 255:256].rearrange(
                        "p a b c -> p (a b c)")), inc=0)
                for dch in range(12, KDI):
                    E("V", lambda e, dch=dch, n=n, o=hof(dch):
                      e.tensor_tensor(
                          out=hall[:, o:o + L], in0=hall[:, o:o + L],
                          in1=cbc3c[:, n, :], op=ALU.mult), inc=0)
                if n == 0:
                    E("V", lambda e: e.tensor_copy(out=yall[:], in_=hall[:]))
                else:
                    E("V", lambda e: e.tensor_tensor(out=yall[:], in0=yall[:],
                                                     in1=hall[:], op=ALU.add))
            for dch in range(KDI):
                E("V", lambda e, dch=dch, o=hof(dch): e.scalar_tensor_tensor(
                    out=yall[:, o:o + L], in0=xc3c[:, dch, :],
                    scalar=dpv[:, dch:dch + 1], in1=yall[:, o:o + L],
                    op0=ALU.mult, op1=ALU.add), inc=0)
            mGate = None
            for dch in range(KDI):
                mGate = E("V", lambda e, dch=dch, o=hof(dch): e.tensor_tensor(
                    out=uv3c[:, dch, :], in0=yall[:, o:o + L],
                    in1=sz3c[:, dch, :], op=ALU.mult),
                    waits=([st["szall"], last["gall_war"]]
                           if dch == 0 else []),
                    inc=(1 if dch == KDI - 1 else 0))
            st["gate"] = mGate
            em.splice("V")
            em.splice("A")

        def emit_out(c):
            st = chst[c]
            gall3 = sl(uv4, c, KDI)
            xv3 = sl(xr4, c, KD)
            x2_ms = {}

            def evac_wout(m, pap, hmm):
                mR = E("V", lambda e, m=m, pap=pap, xv3=xv3: e.tensor_tensor(
                    out=x2v3[:, m, :], in0=xv3[:, m, :], in1=pap,
                    op=ALU.add),
                    waits=[hmm, last["x2_war"], pstate.get("sgn_last"),
                           ("st", em.scnt.get("st", 0))], aux=True)
                x2_ms[m] = mR
                return mR

            CO = [("os0", em.scnt.get("os0", 0)),
                  ("os1", em.scnt.get("os1", 0))]
            stream_mm(w_out16_d, 256, gall3, st["gate"], evac_wout, 4, KDI,
                      CO)
            last["gall_war"] = em.recs["P"][-1]["h"]
            mX2All = hmax(*[x2_ms[m] for m in range(KD)])
            st["resid"] = mX2All

            mXn2 = layer_norm(x2v3, KD, xn3, 1, mX2All, last["xn_war"],
                              aux=True)

            hh_ms = {}
            BS = [("bs0", em.scnt.get("bs0", 0)),
                  ("bs1", em.scnt.get("bs1", 0))]

            def evac_ffn1(m, pap, hmm):
                mE = E("A", lambda e, m=m, pap=pap: e.activation(
                    out=hh3[:, m, :], in_=pap, func=AF.Gelu,
                    scale=al1[:, m:m + 1], bias=b1c[:, m:m + 1]),
                    waits=[hmm, last["hh_war"], ready["alpha"],
                           ready["alpha_v"]] + BS, aux=True)
                hh_ms[m] = mE
                return mE

            stream_mm(w1s_d, 512, xn3, mXn2, evac_ffn1, 8, KD, BS)
            last["xn_war"] = em.recs["P"][-1]["h"]
            last["x2_war"] = em.recs["P"][-1]["h"]
            mHhAll = hmax(*[hh_ms[m] for m in range(KH)])

            ob_ms = {}
            ffb_prev = last["ffb_war"]
            ffb_cur = {}

            def evac_ffn2(m, pap, hmm):
                fslot = m % 2
                mE = E("A", lambda e, pap=pap, fslot=fslot, m=m: e.activation(
                    out=ffb3[:, fslot, :], in_=pap, func=AF.Identity,
                    scale=al2[:, m:m + 1], bias=b2c[:, m:m + 1]),
                    waits=[hmm, ob_ms.get(m - 2, ffb_prev.get(fslot))],
                    aux=True)
                mO = E("V", lambda e, m=m, fslot=fslot: e.tensor_tensor(
                    out=x2v3[:, m, :], in0=x2v3[:, m, :],
                    in1=ffb3[:, fslot, :], op=ALU.add),
                    waits=[mE], aux=True)
                ob_ms[m] = mO
                ffb_cur[fslot] = mO
                return mE

            stream_mm(w2s_d, 128, hh3, mHhAll, evac_ffn2, 8, KH, BS)
            last["hh_war"] = em.recs["P"][-1]["h"]
            last["ffb_war"] = ffb_cur
            mObAll = hmax(*[ob_ms[m] for m in range(KD)])
            DMA("Q", "st", lambda e, c=c: e.dma_start(
                out=out_dst[:, :, c * L:c * L + L], in_=x2v3),
                waits=[mObAll])

        for it in range(NCHUNK + 2):
            if it == 2:
                drain_prep(len(prep_units))
            if it >= 2:
                emit_out(it - 2)
            if it < NCHUNK:
                emit_front(it)
            if it <= 1:
                drain_prep(40)
            if 1 <= it <= NCHUNK:
                emit_scan(it - 1)
            elif it == NCHUNK + 1:
                em.splice("V")
                em.splice("A")

        total_st = em.scnt.get("st", 0)
        em.finalize()
        _EM_BOX.append(em)
        sem = {e: es.enter_context(nc.semaphore("sem" + e)) for e in ENGS}
        for s in sorted(em.scnt):
            sem[s] = es.enter_context(nc.semaphore("sem_" + s))

        def replay(eng_name):
            def body(eng):
                lastw = {}
                for rec in em.seq[eng_name]:
                    for (we, wc) in rec["rwaits"]:
                        if lastw.get(we, -1) < wc:
                            eng.wait_ge(sem[we], wc)
                            lastw[we] = wc
                    inst = rec["fn"](eng)
                    if rec["stream"] is not None:
                        inst.then_inc(sem[rec["stream"]], 16)
                    elif rec["inc"]:
                        inst.then_inc(sem[eng_name], rec["inc"])
                if eng_name == "G" and total_st:
                    eng.wait_ge(sem["st"], total_st)
            return body

        block.tensor(replay("P"))
        block.vector(replay("V"))
        block.scalar(replay("A"))
        block.sync(replay("Q"))
        block.gpsimd(replay("G"))

    return nc


_NC_CACHE = {}


def kernel(L0_attn_x, L0_attn_W_in, L0_attn_conv_w, L0_attn_conv_b, L0_attn_W_x,
           L0_attn_W_dt, L0_attn_b_dt, L0_attn_A_log, L0_attn_D, L0_attn_W_out,
           L0_ffn_w1, L0_ffn_b1, L0_ffn_w2, L0_ffn_b2):
    B = np.asarray(L0_attn_x).shape[0]
    if "nc" not in _NC_CACHE:
        _NC_CACHE["nc"] = build()
    nc = _NC_CACHE["nc"]
    f32 = np.float32
    shared = dict(
        w_in=np.ascontiguousarray(np.asarray(L0_attn_W_in, dtype=f32)),
        conv_w=np.ascontiguousarray(np.asarray(L0_attn_conv_w, dtype=f32)),
        conv_b=np.ascontiguousarray(
            np.asarray(L0_attn_conv_b, dtype=f32)).reshape(16, 128),
        w_x=np.ascontiguousarray(np.asarray(L0_attn_W_x, dtype=f32)),
        w_dt=np.ascontiguousarray(np.asarray(L0_attn_W_dt, dtype=f32)),
        b_dt=np.ascontiguousarray(
            np.asarray(L0_attn_b_dt, dtype=f32)).reshape(16, 128),
        a_log=np.ascontiguousarray(np.asarray(L0_attn_A_log, dtype=f32)),
        dvec=np.ascontiguousarray(
            np.asarray(L0_attn_D, dtype=f32)).reshape(16, 128),
        w_out=np.ascontiguousarray(np.asarray(L0_attn_W_out, dtype=f32)),
        w1t=np.ascontiguousarray(np.asarray(L0_ffn_w1, dtype=f32).T),
        w1r=np.ascontiguousarray(np.asarray(L0_ffn_w1, dtype=f32)),
        w2r=np.ascontiguousarray(np.asarray(L0_ffn_w2, dtype=f32)),
        b1=np.ascontiguousarray(
            np.asarray(L0_ffn_b1, dtype=f32)).reshape(32, 128),
        w2t=np.ascontiguousarray(np.asarray(L0_ffn_w2, dtype=f32).T),
        b2=np.ascontiguousarray(
            np.asarray(L0_ffn_b2, dtype=f32)).reshape(8, 128),
        eye=np.eye(32, dtype=f32),
    )
    in_maps = []
    for b in range(B):
        m = dict(shared)
        m["x"] = np.ascontiguousarray(np.asarray(L0_attn_x[b], dtype=f32).T)
        in_maps.append(m)
    res = run_bass_kernel_spmd(nc, in_maps, core_ids=list(range(B)))
    out = np.empty((B, T, D), dtype=np.float32)
    for b in range(B):
        out[b] = np.asarray(res.results[b]["out"]).T
    return out


# version 29 (final)
# speedup vs baseline: 1.0211x; 1.0211x over previous
import contextlib
import numpy as np
import ml_dtypes
import concourse.bass as bass
import concourse.mybir as mybir
from concourse.bass_utils import run_bass_kernel_spmd

F32 = mybir.dt.float32
F16 = mybir.dt.float16
BF16 = mybir.dt.bfloat16
AF = mybir.ActivationFunctionType
ALU = mybir.AluOpType

D = 1024
DI = 2048
T = 2048
NST = 16
RK = 64
H = 4096
L = 256
NCHUNK = T // L
KD = D // 128
KDI = DI // 128
KH = H // 128
EPS = 1e-5

ENGS = ("P", "V", "A")
QUEUES = ("Q", "G")

_EM_BOX = []


class Handle:
    __slots__ = ("eng", "pos")

    def __init__(self, eng, pos):
        self.eng = eng
        self.pos = pos


def hmax(*hs):
    hs = [h for h in hs if h is not None]
    assert hs and all(h.eng == hs[0].eng for h in hs)
    return max(hs, key=lambda h: h.pos)


class Emitter:

    def __init__(self):
        self.seq = {e: [] for e in ENGS + QUEUES}
        self.bucket = {e: [] for e in ENGS}
        self.recs = {e: [] for e in ENGS}
        self.scnt = {}
        self.gser = 0

    def emit(self, eng, fn, waits=(), inc=1, aux=False):
        rec = {"fn": fn, "waits": [w for w in waits if w is not None],
               "inc": inc, "stream": None, "ser": self.gser}
        self.gser += 1
        self.recs[eng].append(rec)
        h = Handle(eng, len(self.recs[eng]) - 1)
        rec["h"] = h
        (self.bucket[eng] if aux else self.seq[eng]).append(rec)
        return h

    def splice(self, eng):
        self.seq[eng].extend(self.bucket[eng])
        del self.bucket[eng][:]

    def splice_upto(self, eng, ser):
        b = self.bucket[eng]
        n = 0
        while n < len(b) and b[n]["ser"] <= ser:
            n += 1
        self.seq[eng].extend(b[:n])
        del b[:n]

    def dma(self, queue, stream, fn, waits=()):
        rec = {"fn": fn, "waits": [w for w in waits if w is not None],
               "inc": 16, "stream": stream}
        self.scnt[stream] = self.scnt.get(stream, 0) + 16
        self.seq[queue].append(rec)
        return self.scnt[stream]

    def finalize(self):
        for e in ENGS:
            self.splice(e)
            c = 0
            for i, rec in enumerate(self.seq[e]):
                c += rec["inc"]
                rec["cnt"] = c
                rec["ord"] = i
        for e in ENGS + QUEUES:
            for rec in self.seq[e]:
                w = {}
                for item in rec["waits"]:
                    if isinstance(item, Handle):
                        tgt = self.recs[item.eng][item.pos]
                        assert "cnt" in tgt, "waited handle not in seq"
                        assert tgt["inc"] > 0, "waited handle has inc=0"
                        if item.eng == e and e not in QUEUES:
                            assert tgt["ord"] < rec["ord"], \
                                f"same-engine wait out of order on {e}"
                            continue
                        if tgt["cnt"] > w.get(item.eng, 0):
                            w[item.eng] = tgt["cnt"]
                    else:
                        s, c2 = item
                        if c2 > 0 and c2 > w.get(s, 0):
                            w[s] = c2
                rec["rwaits"] = sorted(w.items())


def build():
    nc = bass.Bass("TRN2", target_bir_lowering=False)

    x_d = nc.dram_tensor("x", [D, T], F32, kind="ExternalInput").ap()
    w_in_d = nc.dram_tensor("w_in", [D, 2 * DI], F32, kind="ExternalInput").ap()
    conv_w_d = nc.dram_tensor("conv_w", [DI, 4], F32, kind="ExternalInput").ap()
    conv_b_d = nc.dram_tensor("conv_b", [16, 128], F32, kind="ExternalInput").ap()
    w_x_d = nc.dram_tensor("w_x", [DI, RK + 2 * NST], F32, kind="ExternalInput").ap()
    w_dt_d = nc.dram_tensor("w_dt", [RK, DI], F32, kind="ExternalInput").ap()
    b_dt_d = nc.dram_tensor("b_dt", [16, 128], F32, kind="ExternalInput").ap()
    a_log_d = nc.dram_tensor("a_log", [DI, NST], F32, kind="ExternalInput").ap()
    dvec_d = nc.dram_tensor("dvec", [16, 128], F32, kind="ExternalInput").ap()
    w_out_d = nc.dram_tensor("w_out", [DI, D], F32, kind="ExternalInput").ap()
    w1t_d = nc.dram_tensor("w1t", [D, H], F32, kind="ExternalInput").ap()
    w1r_d = nc.dram_tensor("w1r", [H, D], F32, kind="ExternalInput").ap()
    w2r_d = nc.dram_tensor("w2r", [D, H], F32, kind="ExternalInput").ap()
    b1_d = nc.dram_tensor("b1", [32, 128], F32, kind="ExternalInput").ap()
    w2t_d = nc.dram_tensor("w2t", [H, D], F32, kind="ExternalInput").ap()
    b2_d = nc.dram_tensor("b2", [8, 128], F32, kind="ExternalInput").ap()
    eye_d = nc.dram_tensor("eye", [32, 32], F32, kind="ExternalInput").ap()
    out_d = nc.dram_tensor("out", [D, T], F32, kind="ExternalOutput").ap()
    w_in16_d = nc.dram_tensor("w_in16", [D, 2 * DI], F16, kind="Internal").ap()
    w_out16_d = nc.dram_tensor("w_out16", [DI, D], F16, kind="Internal").ap()
    w1s_d = nc.dram_tensor("w1s", [D, H], F16, kind="Internal").ap()
    w2s_d = nc.dram_tensor("w2s", [H, D], F16, kind="Internal").ap()
    bounce_d = nc.dram_tensor("bounce", [4, 2 * L], F32, kind="Internal").ap()
    bounce16_d = nc.dram_tensor("bounce16", [32, L], F16, kind="Internal").ap()

    em = Emitter()
    ready = {}
    chst = {}

    es = contextlib.ExitStack()
    with es:
        def sbt(name, shape, dt):
            return es.enter_context(nc.sbuf_tensor(name, shape, dt))

        wring = sbt("wring", [128, 2 * 4096], F16)
        wx16 = sbt("wx16", [128, KDI * 96], F16)
        wdt16 = sbt("wdt16", [64, KDI * 128], F16)
        cw = sbt("cw", [128, KDI * 4], F32)
        cb = sbt("cb", [128, KDI], F32)
        bdt = sbt("bdt", [128, KDI], F32)
        dpv = sbt("dpv", [128, KDI], F32)
        asc = sbt("asc", [128, KDI * NST], F32)
        b1c = sbt("b1c", [128, KH], F32)
        b2c = sbt("b2c", [128, KD], F32)
        al1 = sbt("al1", [128, KH], F32)
        al2 = sbt("al2", [128, KD], F32)
        al2p = sbt("al2p", [128, 4 * KD], F32)
        eye_s = sbt("eye_s", [32, 32], F32)
        ones32 = sbt("ones32", [128, 1], F32)
        ones16 = sbt("ones16", [128, 1], F16)
        cst = sbt("cst", [128, 6], F32)

        xr = sbt("xr", [128, 2 * KD * L], F32)
        xn = sbt("xn", [128, KD * L], F16)
        xi = sbt("xi", [128, KDI * (L + 4)], F16)
        xcv = sbt("xcv", [128, 2 * KDI * L], F16)
        szv = sbt("szv", [128, 2 * KDI * L], F16)
        dlt = sbt("dlt", [128, 2 * KDI * L], F16)
        uv = sbt("uv", [128, 2 * KDI * L], F16)
        dtl = sbt("dtl", [64, L], F16)
        bcst = sbt("bcst", [32, L], F16)
        bbc = sbt("bbc", [128, 2 * NST * L], F16)
        cbc = sbt("cbc", [128, 2 * NST * L], BF16)
        QL = 4 * (L + 1)
        dar = sbt("dar", [128, 3 * QL], F16)
        dbu = sbt("dbu", [128, QL], F16)
        hall = sbt("hall", [128, 4 * QL], BF16)
        yall = sbt("yall", [128, 4 * QL], F16)
        x2v = sbt("x2v", [128, KD * L], F32)
        hh = sbt("hh", [128, KH * L], F16)
        ffbuf = sbt("ffbuf", [128, 2 * L], F16)
        lnbc = sbt("lnbc", [128, 2 * L], F32)
        state = sbt("state", [128, NST * KDI], F32)
        r_pair = sbt("r_pair", [1, 2 * L], F32)
        r_sum = sbt("r_sum", [1, L], F32)
        r_sq = sbt("r_sq", [1, L], F32)

        s16a = [hh[:, 0:2048], hh[:, 2048:4096]]
        s16b = [hh[:, 4096:6144], hh[:, 6144:8192]]

        psmm = [es.enter_context(nc.psum_tensor(f"psmm{i}", [128, 512], F32))
                for i in range(4)]
        psX = es.enter_context(nc.psum_tensor("psX", [128, 512], F32))
        psS = es.enter_context(nc.psum_tensor("psS", [128, 512], F32))
        psE = es.enter_context(nc.psum_tensor("psE", [128, 512], F32))
        psE2 = es.enter_context(nc.psum_tensor("psE2", [128, 512], F32))

        block = es.enter_context(nc.Block())

        def v3(t, k):
            return t[:].rearrange("p (a b) -> p a b", a=k)

        xr4 = v3(xr, 2 * KD)
        xn3 = v3(xn, KD)
        xi3 = v3(xi, KDI)
        xcv4 = v3(xcv, 2 * KDI)
        szv4 = v3(szv, 2 * KDI)
        dlt4 = v3(dlt, 2 * KDI)
        uv4 = v3(uv, 2 * KDI)
        bbc4 = v3(bbc, 2 * NST)
        cbc4 = v3(cbc, 2 * NST)
        hall4 = hall[:].rearrange("p (q s t) -> p q s t", q=4, s=4)

        def hof(dch):
            return (dch // 4) * QL + (dch % 4) * (L + 1)
        x2v3 = v3(x2v, KD)
        hh3 = v3(hh, KH)
        lnbc3 = v3(lnbc, 2)
        ffb3 = v3(ffbuf, 2)
        cw3 = v3(cw, KDI)
        asc3 = v3(asc, KDI)
        wx3 = v3(wx16, KDI)
        wdt3 = wdt16[:].rearrange("k (a m) -> k a m", a=KDI)
        x_src = x_d.rearrange("(a p) t -> p a t", p=128)
        out_dst = out_d.rearrange("(a p) t -> p a t", p=128)

        def sl(view4, c, k):
            off = (c % 2) * k
            return view4[:, off:off + k, :]

        slot_free = [None] * 4
        slot_rr = [0]

        def acquire_slot():
            s = slot_rr[0] % 4
            slot_rr[0] += 1
            return s, psmm[s][:, 0:256], slot_free[s]

        ring_free = [None, None]
        ring_rr = [0]

        E = em.emit
        DMA = em.dma

        E("V", lambda e: e.memset(ones32[:], 1.0))
        E("V", lambda e: e.memset(ones16[:], 1.0))
        E("V", lambda e: e.memset(cst[:, 0:1], EPS))
        E("V", lambda e: e.memset(cst[:, 1:2], 1.0))
        E("V", lambda e: e.memset(cst[:, 2:3], -0.5))
        E("V", lambda e: e.memset(cst[:, 3:4], 1.0 / 1024))
        E("V", lambda e: e.memset(cst[:, 4:5], 1.0 / 4096))
        E("V", lambda e: e.memset(state[:], 0.0))
        E("V", lambda e: e.memset(xi[:], 0.0))
        E("V", lambda e: e.memset(dar[:], 0.0))
        DMA("Q", "sq", lambda e: e.dma_start(out=eye_s[:], in_=eye_d))
        DMA("Q", "sq", lambda e: e.dma_start(
            out=cw3, in_=conv_w_d.rearrange("(a p) k -> p a k", p=128)))
        mSqA = DMA("Q", "sq", lambda e: e.dma_start(
            out=asc3, in_=a_log_d.rearrange("(a p) n -> p a n", p=128)))
        SQ = ("sq", mSqA)
        DMA("Q", "sv", lambda e: e.dma_start(out=x2v[0:16, 0:128],
                                             in_=conv_b_d))
        DMA("Q", "sv", lambda e: e.dma_start(out=x2v[0:16, 128:256],
                                             in_=b_dt_d))
        DMA("Q", "sv", lambda e: e.dma_start(out=x2v[0:16, 256:384],
                                             in_=dvec_d))
        DMA("Q", "sv", lambda e: e.dma_start(out=x2v[0:32, 384:512],
                                             in_=b1_d))
        mVl = DMA("Q", "sv", lambda e: e.dma_start(out=x2v[0:8, 512:640],
                                                   in_=b2_d))
        SV = ("sv", mVl)
        E("P", lambda e: e.transpose(out=psS[:, 0:16], in_=x2v[0:16, 0:128],
                                     identity=eye_s[0:16, 0:16]),
          waits=[SQ, SV], inc=0)
        E("P", lambda e: e.transpose(out=psS[:, 16:32], in_=x2v[0:16, 128:256],
                                     identity=eye_s[0:16, 0:16]), inc=0)
        E("P", lambda e: e.transpose(out=psS[:, 32:48], in_=x2v[0:16, 256:384],
                                     identity=eye_s[0:16, 0:16]), inc=0)
        E("P", lambda e: e.transpose(out=psS[:, 48:80], in_=x2v[0:32, 384:512],
                                     identity=eye_s[0:32, 0:32]), inc=0)
        mT5 = E("P", lambda e: e.transpose(out=psS[:, 80:88],
                                           in_=x2v[0:8, 512:640],
                                           identity=eye_s[0:8, 0:8]), inc=1)
        E("V", lambda e: e.tensor_copy(out=cb[:], in_=psS[:, 0:16]),
          waits=[mT5])
        E("V", lambda e: e.tensor_copy(out=bdt[:], in_=psS[:, 16:32]))
        E("V", lambda e: e.tensor_copy(out=dpv[:], in_=psS[:, 32:48]))
        E("V", lambda e: e.tensor_copy(out=b1c[:], in_=psS[:, 48:80]))
        mVecT = E("V", lambda e: e.tensor_copy(out=b2c[:], in_=psS[:, 80:88]))
        ready["vecT"] = mVecT
        mAexp = E("A", lambda e: e.activation(out=asc[:], in_=asc[:],
                                              func=AF.Exp), waits=[SQ])
        E("V", lambda e: e.tensor_scalar_mul(out=asc[:], in0=asc[:],
                                             scalar1=-1.0), waits=[mAexp])
        DMA("G", "pg", lambda e: e.dma_start(
            out=wx3, in_=w_x_d.rearrange("(a p) m -> p a m", p=128)))
        mPg = DMA("G", "pg", lambda e: e.dma_start(
            out=wdt3, in_=w_dt_d.rearrange("k (a m) -> k a m", a=KDI)))
        PG = ("pg", mPg)

        def cast_weight(src_d, dst_d, pieces, cols):
            nsub = max(1, cols // 4096)
            csz = min(cols, 4096)
            for pc in range(pieces):
                for sub in range(nsub):
                    i = pc * nsub + sub
                    s = i % 2
                    ap = wring[:, s * 4096:s * 4096 + csz]
                    mG = DMA("G", f"cl{s}", lambda e, ap=ap, pc=pc, sub=sub:
                             e.dma_start(out=ap,
                                         in_=src_d[pc * 128:(pc + 1) * 128,
                                                   sub * csz:(sub + 1) * csz]),
                             waits=[(f"cs{s}", em.scnt.get(f"cs{s}", 0))])
                    DMA("Q", f"cs{s}", lambda e, ap=ap, pc=pc, sub=sub:
                        e.dma_start(out=dst_d[pc * 128:(pc + 1) * 128,
                                              sub * csz:(sub + 1) * csz],
                                    in_=ap),
                        waits=[(f"cl{s}", mG)])

        cast_weight(w_in_d, w_in16_d, KD, 2 * DI)
        CI = [("cs0", em.scnt.get("cs0", 0)), ("cs1", em.scnt.get("cs1", 0))]

        pstate = {}
        prep_units = []

        def unit_wout(i):
            def go():
                sb = i % 2
                mL = DMA("G", f"ol{sb}", lambda e, i=i, sb=sb: e.dma_start(
                    out=s16b[sb][:, 1024:2048],
                    in_=w_out_d[i * 128:(i + 1) * 128, :]),
                    waits=[(f"os{sb}", em.scnt.get(f"os{sb}", 0))])
                DMA("Q", f"os{sb}", lambda e, i=i, sb=sb: e.dma_start(
                    out=w_out16_d[i * 128:(i + 1) * 128, :],
                    in_=s16b[sb][:, 1024:2048]),
                    waits=[(f"ol{sb}", mL)])
            return go

        def unit_bin(src_d, dst_d, kc, qtr, s):
            def go():
                mL = DMA("Q", f"bl{s}", lambda e: e.dma_start(
                    out=x2v[:, s * 1024:(s + 1) * 1024],
                    in_=src_d[kc * 128:(kc + 1) * 128,
                              qtr * 1024:(qtr + 1) * 1024]),
                    waits=[pstate.get(("sgn", s)), mT5])
                mS = E("A", lambda e: e.activation(
                    out=s16a[s][:, 0:1024], in_=x2v[:, s * 1024:(s + 1) * 1024],
                    func=AF.Sign),
                    waits=[(f"bl{s}", mL),
                           (f"bs{s}", em.scnt.get(f"bs{s}", 0))], aux=True)
                pstate[("sgn", s)] = mS
                pstate["sgn_last"] = mS
                DMA("Q", f"bs{s}", lambda e: e.dma_start(
                    out=dst_d[kc * 128:(kc + 1) * 128,
                              qtr * 1024:(qtr + 1) * 1024],
                    in_=s16a[s][:, 0:1024]),
                    waits=[mS])
            return go

        def unit_alp(dst_ap, src_row):
            def go(dst_ap=dst_ap, src_row=src_row):
                sb = pstate.get("alp_i", 0) % 2
                pstate["alp_i"] = pstate.get("alp_i", 0) + 1
                mL = DMA("G", f"ar{sb}", lambda e: e.dma_start(
                    out=s16b[sb][:, 0:1024], in_=src_row),
                    waits=[pstate.get(("red", sb))])
                mA = E("A", lambda e: e.activation(
                    out=s16b[sb][:, 0:1024], in_=s16b[sb][:, 0:1024],
                    func=AF.Abs), waits=[(f"ar{sb}", mL)], aux=True)
                mR = E("V", lambda e: e.reduce_sum(
                    out=dst_ap, in_=s16b[sb][:, 0:1024],
                    axis=mybir.AxisListType.X), waits=[mA], aux=True)
                pstate[("red", sb)] = mR
            return go

        def unit_alpha_final():
            def go():
                al2p3 = al2p[:].rearrange("p (r q) -> p r q", q=4)
                E("V", lambda e: e.tensor_tensor(
                    out=al2[:], in0=al2p3[:, :, 0], in1=al2p3[:, :, 1],
                    op=ALU.add),
                    waits=[pstate[("red", 0)], pstate[("red", 1)]], aux=True)
                E("V", lambda e: e.tensor_tensor(
                    out=al2p3[:, :, 0], in0=al2p3[:, :, 2],
                    in1=al2p3[:, :, 3], op=ALU.add), aux=True)
                E("V", lambda e: e.tensor_tensor(
                    out=cst[:, 5:6], in0=cst[:, 1:2], in1=cst[:, 1:2],
                    op=ALU.mult), inc=0, aux=True)
                E("V", lambda e: e.tensor_tensor(
                    out=cst[:, 5:6], in0=cst[:, 1:2], in1=cst[:, 1:2],
                    op=ALU.mult), inc=0, aux=True)
                mAd = E("V", lambda e: e.tensor_tensor(
                    out=al2[:], in0=al2[:], in1=al2p3[:, :, 0],
                    op=ALU.add), aux=True)
                mS1 = E("A", lambda e: e.activation(
                    out=al1[:], in_=al1[:], func=AF.Identity,
                    scale=cst[:, 3:4]), waits=[mAd], aux=True)
                mS2 = E("A", lambda e: e.activation(
                    out=al2[:], in_=al2[:], func=AF.Identity,
                    scale=cst[:, 4:5]), aux=True)
                ready["alpha"] = mS2
                ready["alpha_v"] = mAd
            return go

        LW = [unit_wout(i) for i in range(KDI)]
        LA = [unit_alp(al1[:, r:r + 1], w1r_d[r * 128:(r + 1) * 128, :])
              for r in range(KH)]
        LA += [unit_alp(al2p[:, r * 4 + q:r * 4 + q + 1],
                        w2r_d[r * 128:(r + 1) * 128,
                              q * 1024:(q + 1) * 1024])
               for r in range(KD) for q in range(4)]
        LA.append(unit_alpha_final())
        LB = []
        nbin = 0
        for kc in range(KD):
            for qtr in range(4):
                LB.append(unit_bin(w1t_d, w1s_d, kc, qtr, nbin % 2))
                nbin += 1
        for kc in range(KH):
            LB.append(unit_bin(w2t_d, w2s_d, kc, 0, nbin % 2))
            nbin += 1
        tagged = []
        for lst in (LW, LA, LB):
            for j, u in enumerate(lst):
                tagged.append(((j + 0.5) / len(lst), len(tagged), u))
        tagged.sort()
        prep_units = [t[2] for t in tagged]
        pidx = [0]

        def drain_prep(k):
            n = min(k, len(prep_units) - pidx[0])
            for i in range(n):
                prep_units[pidx[0] + i]()
            pidx[0] += n

        def stream_mm(src16_d, mg_cols, rhs3, rhs_wait, evac_fn, mg_count,
                      kparts, src_waits):
            src3 = src16_d.rearrange("(a p) m -> p a m", p=128)
            for mg in range(mg_count):
                s = ring_rr[0] % 2
                ring_rr[0] += 1
                ap = wring[:, s * 4096:(s + 1) * 4096]
                stream = "wr%d" % s
                mLd = DMA("Q", stream, lambda e, mg=mg, ap=ap: e.dma_start(
                    out=ap[:, 0:kparts * mg_cols].rearrange(
                        "p (a m) -> p a m", a=kparts),
                    in_=src3[:, :, mg * mg_cols:(mg + 1) * mg_cols]),
                    waits=[ring_free[s]] + src_waits)
                n_m = mg_cols // 128
                for ml in range(n_m):
                    m = mg * n_m + ml
                    slq, pap, sfree = acquire_slot()
                    hmm = None
                    for kc in range(kparts):
                        off = kc * mg_cols + ml * 128
                        hmm = E("P", lambda e, ap=ap, off=off, pap=pap, kc=kc:
                                e.matmul(out=pap, lhsT=ap[:, off:off + 128],
                                         rhs=rhs3[:, kc, :],
                                         start=(kc == 0),
                                         stop=(kc == kparts - 1)),
                                waits=([(stream, mLd), rhs_wait, sfree]
                                       if kc == 0 else []),
                                inc=(1 if kc == kparts - 1 else 0))
                    slot_free[slq] = evac_fn(m, pap, hmm)
                ring_free[s] = hmm

        def layer_norm(xsrc3, kd, out3, brow0, xwait, war_xn, aux):
            mSq = E("A", lambda e, xsrc3=xsrc3: e.activation(
                out=xn3, in_=xsrc3, func=AF.Square),
                waits=[xwait, ready.get("t1_done"), war_xn], aux=aux)
            for kc in range(kd):
                E("P", lambda e, kc=kc, xsrc3=xsrc3: e.matmul(
                    out=psS[0:1, 0:L], lhsT=ones32[:], rhs=xsrc3[:, kc, :],
                    start=(kc == 0), stop=(kc == kd - 1)),
                    waits=([xwait, ready.get("psS_war", ready["vecT"])]
                           if kc == 0 else []), inc=0)
            hstat = None
            for kc in range(kd):
                hstat = E("P", lambda e, kc=kc: e.matmul(
                    out=psS[0:1, 256:256 + L], lhsT=ones16[:],
                    rhs=xn3[:, kc, :],
                    start=(kc == 0), stop=(kc == kd - 1)),
                    waits=([mSq] if kc == 0 else []),
                    inc=(1 if kc == kd - 1 else 0))
            E("V", lambda e: e.tensor_copy(out=r_sum[:], in_=psS[0:1, 0:L]),
              waits=[hstat, ("rp", ready.get("rows_war", 0))], aux=aux)
            mRows = E("V", lambda e: e.tensor_copy(
                out=r_sq[:], in_=psS[0:1, 256:256 + L]), aux=aux)
            ready["psS_war"] = mRows
            E("V", lambda e: e.tensor_scalar_mul(out=r_sum[:], in0=r_sum[:],
                                                 scalar1=1.0 / D), aux=aux)
            E("V", lambda e: e.tensor_mul(out=r_pair[:, 0:L], in0=r_sum[:],
                                          in1=r_sum[:]), aux=aux)
            mVar = E("V", lambda e: e.scalar_tensor_tensor(
                out=r_sq[:], in0=r_sq[:], scalar=1.0 / D,
                in1=r_pair[:, 0:L],
                op0=ALU.mult, op1=ALU.subtract), aux=aux)
            E("A", lambda e: e.activation(out=r_sq[:], in_=r_sq[:],
                                          func=AF.Ln, bias=cst[0:1, 0:1]),
              waits=[mVar], aux=aux)
            mRstd = E("A", lambda e: e.activation(out=r_pair[:, 0:L],
                                                  in_=r_sq[:],
                                                  func=AF.Exp,
                                                  scale=cst[0:1, 2:3]),
                      aux=aux)
            mMr = E("V", lambda e: e.tensor_mul(out=r_pair[:, L:2 * L],
                                                in0=r_sum[:],
                                                in1=r_pair[:, 0:L]),
                    waits=[mRstd], aux=aux)
            mRo = DMA("G", "ro", lambda e, brow0=brow0: e.dma_start(
                out=bounce_d[brow0:brow0 + 1, :], in_=r_pair[:]),
                waits=[mMr, mRstd])
            mRep = DMA("G", "rp", lambda e, brow0=brow0: e.dma_start(
                out=lnbc[:],
                in_=bounce_d[brow0:brow0 + 1, :].partition_broadcast(128)),
                waits=[("ro", mRo), ("rp", em.scnt.get("rp", 0)),
                       ready.get("lnbc_war")])
            ready["rows_war"] = mRep
            E("V", lambda e, xsrc3=xsrc3, kd=kd: e.tensor_tensor(
                out=xn3, in0=xsrc3,
                in1=lnbc3[:, 0:1, :].broadcast_to([128, kd, L]),
                op=ALU.mult),
                waits=[("rp", mRep)], aux=aux)
            mXn = E("V", lambda e, out3=out3, kd=kd: e.tensor_tensor(
                out=out3, in0=xn3,
                in1=lnbc3[:, 1:2, :].broadcast_to([128, kd, L]),
                op=ALU.subtract), aux=aux)
            ready["lnbc_war"] = mXn
            ready["t1_done"] = mXn
            return mXn

        scan_ms_all = []
        dA_all = []
        last = {"psX_war": None, "bcst_war": 0, "gall_war": None,
                "x2_war": None, "hh_war": None, "xn_war": None,
                "ffb_war": {}}

        def emit_front(c):
            st = chst.setdefault(c, {})
            p2 = chst.get(c - 2, {})
            p1 = chst.get(c - 1, {})
            xv3 = sl(xr4, c, KD)
            mX = DMA("Q", "xl", lambda e, c=c: e.dma_start(
                out=sl(xr4, c, KD), in_=x_src[:, :, c * L:c * L + L]),
                waits=[p2.get("resid")])
            mXn = layer_norm(xv3, KD, xn3, 0, ("xl", mX), last["xn_war"],
                             aux=True)
            st["xn"] = mXn

            evac_ms = {}

            def evac_win(m, pap, hmm):
                if m < KDI:
                    E("A", lambda e, m=m, pap=pap: e.activation(
                        out=xi3[:, m, 4:4 + L], in_=pap, func=AF.Exp),
                        waits=[hmm, p1.get("hist")], aux=True)
                    mE = E("A", lambda e, m=m: e.activation(
                        out=xi3[:, m, 4:4 + L], in_=xi3[:, m, 4:4 + L],
                        func=AF.Ln), aux=True)
                else:
                    mE = E("A", lambda e, m=m, pap=pap, c=c: e.activation(
                        out=sl(szv4, c, KDI)[:, m - KDI, :], in_=pap,
                        func=AF.Silu),
                        waits=[hmm, p2.get("gate")], aux=True)
                evac_ms[m] = mE
                return mE

            stream_mm(w_in16_d, 512, xn3, mXn, evac_win, 8, KD, CI)
            last["xn_war"] = em.recs["P"][-1]["h"]

            xc3c = sl(xcv4, c, KDI)
            for dch in range(KDI):
                for k in range(4):
                    if k == 0:
                        wv = [evac_ms[dch], SQ,
                              (p2.get("gate") if dch == 0 else None)]
                        E("V", lambda e, dch=dch, xc3c=xc3c:
                          e.tensor_scalar_mul(
                              out=xc3c[:, dch, :],
                              in0=xi3[:, dch, 1:1 + L],
                              scalar1=cw3[:, dch, 0:1]), waits=wv, inc=0,
                          aux=True)
                    else:
                        E("V", lambda e, dch=dch, k=k, xc3c=xc3c:
                          e.scalar_tensor_tensor(
                              out=xc3c[:, dch, :],
                              in0=xi3[:, dch, 1 + k:1 + k + L],
                              scalar=cw3[:, dch, k:k + 1],
                              in1=xc3c[:, dch, :],
                              op0=ALU.mult, op1=ALU.add),
                          inc=(1 if k == 3 else 0), aux=True)
                mAcc = em.recs["V"][-1]["h"]
                mXc = E("A", lambda e, dch=dch, xc3c=xc3c:
                        e.activation(out=xc3c[:, dch, :],
                                     in_=xc3c[:, dch, :], func=AF.Silu,
                                     bias=cb[:, dch:dch + 1]),
                        waits=[mAcc], aux=True)
                evac_ms[("xc", dch)] = mXc
            mXcAll = hmax(*[evac_ms[("xc", d_)] for d_ in range(KDI)])
            st["szall"] = hmax(*[evac_ms[m] for m in range(KDI, 2 * KDI)])
            mHist = E("V", lambda e: e.tensor_copy(
                out=xi3[:, :, 1:4], in_=xi3[:, :, 1 + L:4 + L]),
                waits=[hmax(*[evac_ms[m] for m in range(KDI)])], aux=True)
            st["hist"] = mHist

            hxd = None
            for kc in range(KDI):
                hxd = E("P", lambda e, kc=kc, xc3c=xc3c: e.matmul(
                    out=psX[0:96, 0:L], lhsT=wx3[:, kc, :],
                    rhs=xc3c[:, kc, :],
                    start=(kc == 0), stop=(kc == KDI - 1)),
                    waits=([mXcAll, PG, last["psX_war"]] if kc == 0 else []),
                    inc=(1 if kc == KDI - 1 else 0))
            dtlc = dtl[:, 0:L]
            E("A", lambda e, dtlc=dtlc: e.activation(
                out=dtlc, in_=psX[0:64, 0:L], func=AF.Exp),
                waits=[hxd, p1.get("wdt_end")], aux=True)
            mDtl = E("A", lambda e, dtlc=dtlc: e.activation(
                out=dtlc, in_=dtlc, func=AF.Ln), aux=True)
            E("A", lambda e: e.activation(out=bcst[:],
                                          in_=psX[64:96, 0:L],
                                          func=AF.Exp),
              waits=[("bo", last["bcst_war"])], aux=True)
            mBcst = E("A", lambda e: e.activation(out=bcst[:], in_=bcst[:],
                                                  func=AF.Ln), aux=True)
            last["psX_war"] = mBcst
            mBo = DMA("G", "bo", lambda e: e.dma_start(
                out=bounce16_d, in_=bcst[:]),
                waits=[mBcst, ("rp", p1.get("crep", 0))])
            last["bcst_war"] = mBo
            DMA("G", "rp", lambda e, c=c: e.dma_start(
                out=sl(bbc4, c, NST),
                in_=bounce16_d[0:16, :].partition_broadcast(128)),
                waits=[("bo", mBo), ("rp", em.scnt.get("rp", 0)),
                       p2.get("gate")])
            mCrep = DMA("G", "rp", lambda e, c=c: e.dma_start(
                out=sl(cbc4, c, NST),
                in_=bounce16_d[16:32, :].partition_broadcast(128)),
                waits=[("rp", em.scnt.get("rp", 0)), p2.get("gate")])
            st["crep"] = mCrep

            dlt3c = sl(dlt4, c, KDI)
            hdt = None
            for dch in range(KDI):
                pap = (psE if dch % 2 == 0 else psE2)[:, 0:L]
                hdt = E("P", lambda e, pap=pap, dtlc=dtlc, dch=dch: e.matmul(
                    out=pap, lhsT=wdt3[:, dch, :], rhs=dtlc,
                    start=True, stop=True),
                    waits=[mDtl, PG, ready.get(("sp", c, dch - 2))], inc=1)
                mEx = E("A", lambda e, pap=pap, dch=dch, dlt3c=dlt3c:
                        e.activation(out=dlt3c[:, dch, :], in_=pap,
                                     func=AF.Exp, bias=bdt[:, dch:dch + 1]),
                        waits=[hdt, (p2.get("gate") if dch == 0 else None)],
                        aux=True)
                ready[("sp", c, dch)] = mEx
            for dch in range(KDI):
                mDl = E("A", lambda e, dch=dch, dlt3c=dlt3c: e.activation(
                    out=dlt3c[:, dch, :], in_=dlt3c[:, dch, :], func=AF.Ln,
                    bias=cst[:, 1:2]), aux=True)
            st["dltall"] = mDl
            st["wdt_end"] = hdt

            mU = E("V", lambda e, c=c: e.tensor_tensor(
                out=sl(uv4, c, KDI), in0=sl(dlt4, c, KDI),
                in1=sl(xcv4, c, KDI), op=ALU.mult),
                waits=[st["dltall"]], aux=True)
            st["mU"] = mU
            st["eser"] = em.gser

        def emit_scan(c):
            st = chst[c]
            base_g = c * 256
            dlt3c = sl(dlt4, c, KDI)
            uv3c = sl(uv4, c, KDI)
            bbc3c = sl(bbc4, c, NST)
            cbc3c = sl(cbc4, c, NST)
            xc3c = sl(xcv4, c, KDI)

            em.splice_upto("V", st["eser"])
            em.splice_upto("A", st["eser"])

            sers = sorted(r["ser"] for e2 in ("V", "A")
                          for r in em.bucket[e2])

            def frontier(k):
                if not sers:
                    return -1
                idx = min(len(sers), (len(sers) * k) // 128)
                return sers[idx - 1] if idx > 0 else -1

            sz3c = sl(szv4, c, KDI)
            base_q = c * 64

            def emit_dA(gq):
                lq = gq % 64
                n_, j_ = divmod(lq, 4)
                rs = gq % 3
                wv = [scan_ms_all[gq - 3] if gq >= 3 else st["mU"]]
                if lq == 0:
                    wv.append(st["dltall"])
                m = None
                for s in range(4):
                    off = rs * QL + s * (L + 1)
                    m = E("A", lambda e, n_=n_, dch_=j_ * 4 + s, off=off:
                          e.activation(out=dar[:, off:off + L],
                                       in_=dlt3c[:, dch_, :], func=AF.Exp,
                                       scale=asc3[:, dch_, n_:n_ + 1]),
                          waits=(wv if s == 0 else []))
                dA_all.append(m)

            def emit_dbu(gq):
                lq = gq % 64
                n_, j_ = divmod(lq, 4)
                col0 = n_ * KDI + j_ * 4
                wv = [("rp", st["crep"]), st["mU"]] if lq == 0 else []
                E("V", lambda e, col0=col0: e.tensor_copy(
                    out=dbu[:].rearrange(
                        "p (s t) -> p s t", s=4)[:, 0:3, 256:257].rearrange(
                        "p a b -> p (a b)"),
                    in_=state[:, col0 + 1:col0 + 4]), waits=wv, inc=0)
                for s in range(4):
                    off = s * (L + 1)
                    E("V", lambda e, n_=n_, dch_=j_ * 4 + s, off=off:
                      e.tensor_tensor(
                          out=dbu[:, off:off + L], in0=uv3c[:, dch_, :],
                          in1=bbc3c[:, n_, :], op=ALU.mult), inc=0)

            for i in range(3):
                emit_dA(base_q + i)
            emit_dbu(base_q)
            for n in range(NST):
                for j in range(4):
                    gq = base_q + n * 4 + j
                    mScan = E("V", lambda e, j=j, rs=(gq % 3),
                              col=n * KDI + j * 4:
                              e.tensor_tensor_scan(
                                  out=hall[:, j * QL:(j + 1) * QL],
                                  data0=dar[:, rs * QL:(rs + 1) * QL],
                                  data1=dbu[:],
                                  initial=state[:, col:col + 1],
                                  op0=ALU.mult, op1=ALU.add),
                              waits=[dA_all[gq]])
                    scan_ms_all.append(mScan)
                    f = frontier((n * 4 + j) * 2 + 1)
                    em.splice_upto("V", f)
                    em.splice_upto("A", f)
                    if gq + 1 < base_q + 64:
                        emit_dbu(gq + 1)
                    if gq + 3 < base_q + 64:
                        emit_dA(gq + 3)
                    f = frontier((n * 4 + j + 1) * 2)
                    em.splice_upto("V", f)
                    em.splice_upto("A", f)
                E("V", lambda e, n=n: e.tensor_copy(
                    out=state[:, n * KDI:n * KDI + 12],
                    in_=hall4[:, 0:3, :, 255:256].rearrange(
                        "p a b c -> p (a b c)")), inc=0)
                for dch in range(12):
                    E("V", lambda e, dch=dch, n=n, o=hof(dch):
                      e.tensor_tensor(
                          out=hall[:, o:o + L], in0=hall[:, o:o + L],
                          in1=cbc3c[:, n, :], op=ALU.mult), inc=0)
                E("V", lambda e, n=n: e.tensor_copy(
                    out=state[:, n * KDI + 12:(n + 1) * KDI],
                    in_=hall4[:, 3:4, :, 255:256].rearrange(
                        "p a b c -> p (a b c)")), inc=0)
                for dch in range(12, KDI):
                    E("V", lambda e, dch=dch, n=n, o=hof(dch):
                      e.tensor_tensor(
                          out=hall[:, o:o + L], in0=hall[:, o:o + L],
                          in1=cbc3c[:, n, :], op=ALU.mult), inc=0)
                if n == 0:
                    E("V", lambda e: e.tensor_copy(out=yall[:], in_=hall[:]))
                else:
                    E("V", lambda e: e.tensor_tensor(out=yall[:], in0=yall[:],
                                                     in1=hall[:], op=ALU.add))
            for dch in range(KDI):
                E("V", lambda e, dch=dch, o=hof(dch): e.scalar_tensor_tensor(
                    out=yall[:, o:o + L], in0=xc3c[:, dch, :],
                    scalar=dpv[:, dch:dch + 1], in1=yall[:, o:o + L],
                    op0=ALU.mult, op1=ALU.add), inc=0)
            mGate = None
            for dch in range(KDI):
                mGate = E("V", lambda e, dch=dch, o=hof(dch): e.tensor_tensor(
                    out=uv3c[:, dch, :], in0=yall[:, o:o + L],
                    in1=sz3c[:, dch, :], op=ALU.mult),
                    waits=([st["szall"], last["gall_war"]]
                           if dch == 0 else []),
                    inc=(1 if dch == KDI - 1 else 0))
            st["gate"] = mGate
            em.splice("V")
            em.splice("A")

        def emit_out(c):
            st = chst[c]
            gall3 = sl(uv4, c, KDI)
            xv3 = sl(xr4, c, KD)
            x2_ms = {}

            def evac_wout(m, pap, hmm):
                mR = E("V", lambda e, m=m, pap=pap, xv3=xv3: e.tensor_tensor(
                    out=x2v3[:, m, :], in0=xv3[:, m, :], in1=pap,
                    op=ALU.add),
                    waits=[hmm, last["x2_war"], pstate.get("sgn_last"),
                           ("st", em.scnt.get("st", 0))], aux=True)
                x2_ms[m] = mR
                return mR

            CO = [("os0", em.scnt.get("os0", 0)),
                  ("os1", em.scnt.get("os1", 0))]
            stream_mm(w_out16_d, 256, gall3, st["gate"], evac_wout, 4, KDI,
                      CO)
            last["gall_war"] = em.recs["P"][-1]["h"]
            mX2All = hmax(*[x2_ms[m] for m in range(KD)])
            st["resid"] = mX2All

            mXn2 = layer_norm(x2v3, KD, xn3, 1, mX2All, last["xn_war"],
                              aux=True)

            hh_ms = {}
            BS = [("bs0", em.scnt.get("bs0", 0)),
                  ("bs1", em.scnt.get("bs1", 0))]

            def evac_ffn1(m, pap, hmm):
                mE = E("A", lambda e, m=m, pap=pap: e.activation(
                    out=hh3[:, m, :], in_=pap, func=AF.Gelu,
                    scale=al1[:, m:m + 1], bias=b1c[:, m:m + 1]),
                    waits=[hmm, last["hh_war"], ready["alpha"],
                           ready["alpha_v"]] + BS, aux=True)
                hh_ms[m] = mE
                return mE

            stream_mm(w1s_d, 512, xn3, mXn2, evac_ffn1, 8, KD, BS)
            last["xn_war"] = em.recs["P"][-1]["h"]
            last["x2_war"] = em.recs["P"][-1]["h"]
            mHhAll = hmax(*[hh_ms[m] for m in range(KH)])

            ob_ms = {}
            ffb_prev = last["ffb_war"]
            ffb_cur = {}

            def evac_ffn2(m, pap, hmm):
                fslot = m % 2
                mE = E("A", lambda e, pap=pap, fslot=fslot, m=m: e.activation(
                    out=ffb3[:, fslot, :], in_=pap, func=AF.Identity,
                    scale=al2[:, m:m + 1], bias=b2c[:, m:m + 1]),
                    waits=[hmm, ob_ms.get(m - 2, ffb_prev.get(fslot))],
                    aux=True)
                mO = E("V", lambda e, m=m, fslot=fslot: e.tensor_tensor(
                    out=x2v3[:, m, :], in0=x2v3[:, m, :],
                    in1=ffb3[:, fslot, :], op=ALU.add),
                    waits=[mE], aux=True)
                ob_ms[m] = mO
                ffb_cur[fslot] = mO
                return mE

            stream_mm(w2s_d, 128, hh3, mHhAll, evac_ffn2, 8, KH, BS)
            last["hh_war"] = em.recs["P"][-1]["h"]
            last["ffb_war"] = ffb_cur
            mObAll = hmax(*[ob_ms[m] for m in range(KD)])
            DMA("Q", "st", lambda e, c=c: e.dma_start(
                out=out_dst[:, :, c * L:c * L + L], in_=x2v3),
                waits=[mObAll])

        for it in range(NCHUNK + 2):
            if it == 2:
                drain_prep(len(prep_units))
            if it >= 2:
                emit_out(it - 2)
            if it < NCHUNK:
                emit_front(it)
            if it <= 1:
                drain_prep(40)
            if 1 <= it <= NCHUNK:
                emit_scan(it - 1)
            elif it == NCHUNK + 1:
                em.splice("V")
                em.splice("A")

        total_st = em.scnt.get("st", 0)
        em.finalize()
        _EM_BOX.append(em)
        sem = {e: es.enter_context(nc.semaphore("sem" + e)) for e in ENGS}
        for s in sorted(em.scnt):
            sem[s] = es.enter_context(nc.semaphore("sem_" + s))

        def replay(eng_name):
            def body(eng):
                lastw = {}
                for rec in em.seq[eng_name]:
                    for (we, wc) in rec["rwaits"]:
                        if lastw.get(we, -1) < wc:
                            eng.wait_ge(sem[we], wc)
                            lastw[we] = wc
                    inst = rec["fn"](eng)
                    if rec["stream"] is not None:
                        inst.then_inc(sem[rec["stream"]], 16)
                    elif rec["inc"]:
                        inst.then_inc(sem[eng_name], rec["inc"])
                if eng_name == "G" and total_st:
                    eng.wait_ge(sem["st"], total_st)
            return body

        block.tensor(replay("P"))
        block.vector(replay("V"))
        block.scalar(replay("A"))
        block.sync(replay("Q"))
        block.gpsimd(replay("G"))

    return nc


_NC_CACHE = {}


def kernel(L0_attn_x, L0_attn_W_in, L0_attn_conv_w, L0_attn_conv_b, L0_attn_W_x,
           L0_attn_W_dt, L0_attn_b_dt, L0_attn_A_log, L0_attn_D, L0_attn_W_out,
           L0_ffn_w1, L0_ffn_b1, L0_ffn_w2, L0_ffn_b2):
    B = np.asarray(L0_attn_x).shape[0]
    if "nc" not in _NC_CACHE:
        _NC_CACHE["nc"] = build()
    nc = _NC_CACHE["nc"]
    f32 = np.float32
    shared = dict(
        w_in=np.ascontiguousarray(np.asarray(L0_attn_W_in, dtype=f32)),
        conv_w=np.ascontiguousarray(np.asarray(L0_attn_conv_w, dtype=f32)),
        conv_b=np.ascontiguousarray(
            np.asarray(L0_attn_conv_b, dtype=f32)).reshape(16, 128),
        w_x=np.ascontiguousarray(np.asarray(L0_attn_W_x, dtype=f32)),
        w_dt=np.ascontiguousarray(np.asarray(L0_attn_W_dt, dtype=f32)),
        b_dt=np.ascontiguousarray(
            np.asarray(L0_attn_b_dt, dtype=f32)).reshape(16, 128),
        a_log=np.ascontiguousarray(np.asarray(L0_attn_A_log, dtype=f32)),
        dvec=np.ascontiguousarray(
            np.asarray(L0_attn_D, dtype=f32)).reshape(16, 128),
        w_out=np.ascontiguousarray(np.asarray(L0_attn_W_out, dtype=f32)),
        w1t=np.ascontiguousarray(np.asarray(L0_ffn_w1, dtype=f32).T),
        w1r=np.ascontiguousarray(np.asarray(L0_ffn_w1, dtype=f32)),
        w2r=np.ascontiguousarray(np.asarray(L0_ffn_w2, dtype=f32)),
        b1=np.ascontiguousarray(
            np.asarray(L0_ffn_b1, dtype=f32)).reshape(32, 128),
        w2t=np.ascontiguousarray(np.asarray(L0_ffn_w2, dtype=f32).T),
        b2=np.ascontiguousarray(
            np.asarray(L0_ffn_b2, dtype=f32)).reshape(8, 128),
        eye=np.eye(32, dtype=f32),
    )
    in_maps = []
    for b in range(B):
        m = dict(shared)
        m["x"] = np.ascontiguousarray(np.asarray(L0_attn_x[b], dtype=f32).T)
        in_maps.append(m)
    res = run_bass_kernel_spmd(nc, in_maps, core_ids=list(range(B)))
    out = np.empty((B, T, D), dtype=np.float32)
    for b in range(B):
        out[b] = np.asarray(res.results[b]["out"]).T
    return out
